# version 02 (final)
# baseline (speedup 1.0000x reference)
import contextlib
import numpy as np
import concourse.bass as bass
import concourse.mybir as mybir
from concourse.bass_utils import run_bass_kernel_spmd

F32 = mybir.dt.float32
BF16 = mybir.dt.bfloat16
AF = mybir.ActivationFunctionType
ALU = mybir.AluOpType

D = 2048
T = 2048
NF = 16
CH = 256
NCH = 8
DIN = 4096
PROJ_PAD = 8576
EPS = 1e-5

ENGINES = ("pe", "act", "dve", "pool", "sp")


class Prog:
    def __init__(self, nc):
        self.nc = nc
        self.ops = {e: [] for e in ENGINES}
        self.count = {e: 0 for e in ENGINES}
        self.count["cc"] = 0
        for s in ("dwl", "dms", "dv1", "dv2", "ds0", "ds1", "ds2", "ds3",
                  "dt0", "dt1", "dt2", "dt3", "dtl", "dp0", "dp1", "dxb", "dxt",
                  "dss", "dfl", "dlx", "dlp", "dlg", "dg1", "dcc", "dou"):
            self.count[s] = 0
        self.sems = {}

    SAFE_ENGINES = ("act", "dve", "pool")

    def op(self, engine, builder, inc=0, sem=None):
        if engine in self.SAFE_ENGINES and sem is None:
            prev = self.count[engine]
            if prev:
                self.wait(engine, (engine, prev))
            inc = max(inc, 1)
        if inc:
            s = sem or engine
            self.count[s] += inc
            val = self.count[s]

            def fn(eng, b=builder, s=s, a=inc):
                b(eng).then_inc(self.sems[s], a)
            self.ops[engine].append(fn)
            return (s, val)
        self.ops[engine].append(builder)
        return None

    def now(self, engine):
        return (engine, self.count[engine])

    def wait(self, engine, token):
        if token is None:
            return
        s, v = token

        def fn(eng, s=s, v=v):
            eng.wait_ge(self.sems[s], v)
        self.ops[engine].append(fn)

    def waits(self, engine, tokens):
        best = {}
        for t in tokens:
            if t is None:
                continue
            s, v = t
            if best.get(s, -1) < v:
                best[s] = v
        for s, v in sorted(best.items()):
            self.wait(engine, (s, v))

    DMA_SEMS = ("dwl", "dms", "dv1", "dv2", "ds0", "ds1", "ds2", "ds3",
                "dt0", "dt1", "dt2", "dt3", "dtl", "dp0", "dp1", "dxb", "dxt",
                "dss", "dfl", "dlx", "dlp", "dlg", "dg1", "dcc", "dou")

    def dma(self, queue, out, in_, sem="dwl"):
        return self.op(queue, lambda eng: eng.dma_start(out=out, in_=in_), inc=16, sem=sem)

    def dma_fence(self, engine):
        for s in self.DMA_SEMS:
            if self.count.get(s, 0):
                self.wait(engine, (s, self.count[s]))

    def phase_fence(self):
        toks = [self.now(e) for e in ("pe", "act", "dve", "pool")]
        for e in ("pe", "act", "dve", "pool", "sp"):
            self.waits(e, [t for t in toks if t[0] != e])

    def replay(self, block):
        def mk(e):
            def body(eng):
                for fn in self.ops[e]:
                    fn(eng)
            return body
        block.sync(mk("sp"))
        block.tensor(mk("pe"))
        block.scalar(mk("act"))
        block.vector(mk("dve"))
        block.gpsimd(mk("pool"))


class Banks:
    def __init__(self, P):
        self.P = P
        self.t = [[] for _ in range(8)]

    def guard(self, i):
        self.P.waits("pe", self.t[i])
        self.t[i] = []

    def reader(self, i, tok):
        self.t[i].append(tok)


def build_program(nblocks=4, dbg=None, dbg2=None):
    nc = bass.Bass("TRN2", target_bir_lowering=False)
    dt = nc.dram_tensor

    xT = dt("xT", [D, T + 2], F32, kind="ExternalInput").ap()
    wuT = dt("wuT", [D, D], F32, kind="ExternalInput").ap()
    wvT = dt("wvT", [D, D], F32, kind="ExternalInput").ap()
    wg1T = dt("wg1T", [D, 512], F32, kind="ExternalInput").ap()
    wg2T = dt("wg2T", [512, D], F32, kind="ExternalInput").ap()
    bL0a = dt("bL0a", [128, 116], F32, kind="ExternalInput").ap()
    gw0 = dt("gw0", [D, 8], F32, kind="ExternalInput").ap()
    w10 = dt("w10", [8, D, 1024], F32, kind="ExternalInput").ap()
    b10 = dt("b10", [128, 80], F32, kind="ExternalInput").ap()
    w20 = dt("w20", [8, 1024, D], F32, kind="ExternalInput").ap()
    b20 = dt("b20", [8, D], F32, kind="ExternalInput").ap()
    sw1 = dt("sw1", [D, D], F32, kind="ExternalInput").ap()
    sw2 = dt("sw2", [D, D], F32, kind="ExternalInput").ap()
    sb2c = dt("sb2c", [128, 16], F32, kind="ExternalInput").ap()
    ipT = dt("ipT", [D, PROJ_PAD], F32, kind="ExternalInput").ap()
    cw1 = dt("cw1", [128, 136], F32, kind="ExternalInput").ap()
    cb1 = dt("cb1", [128, 34], F32, kind="ExternalInput").ap()
    dtb = dt("dtb", [64, 1], F32, kind="ExternalInput").ap()
    alog = dt("alog", [64, 1], F32, kind="ExternalInput").ap()
    nwc = dt("nwc", [128, 32], F32, kind="ExternalInput").ap()
    dexp = dt("dexp", [1, DIN], F32, kind="ExternalInput").ap()
    opT = dt("opT", [DIN, D], F32, kind="ExternalInput").ap()
    gw1 = dt("gw1", [D, 8], F32, kind="ExternalInput").ap()
    wdT = dt("wdT", [8, D, 512], F32, kind="ExternalInput").ap()
    bd1 = dt("bd1", [128, 32], F32, kind="ExternalInput").ap()
    wuT1 = dt("wuT1", [8, 512, D], F32, kind="ExternalInput").ap()
    bu1 = dt("bu1", [8, D], F32, kind="ExternalInput").ap()
    wsdT = dt("wsdT", [D, 512], F32, kind="ExternalInput").ap()
    bsdc = dt("bsdc", [128, 4], F32, kind="ExternalInput").ap()
    wsuT = dt("wsuT", [512, D], F32, kind="ExternalInput").ap()
    bsuc = dt("bsuc", [128, 16], F32, kind="ExternalInput").ap()
    identf = dt("identf", [128, 128], F32, kind="ExternalInput").ap()
    expmf = dt("expmf", [64, DIN], F32, kind="ExternalInput").ap()
    trif = dt("trif", [128, 2 * CH], F32, kind="ExternalInput").ap()
    selmf = dt("selmf", [8, 1024], F32, kind="ExternalInput").ap()
    mcum = dt("mcum", [128, 8], F32, kind="ExternalInput").ap()
    mtail = dt("mtail", [128, 8], F32, kind="ExternalInput").ap()
    gcolf = dt("gcolf", [128, 8], F32, kind="ExternalInput").ap()
    grow = dt("grow", [1, 16], F32, kind="ExternalInput").ap()

    out_fm = dt("out_fm", [D, T], F32, kind="ExternalOutput").ap()

    p_spill = dt("p_spill", [D, T], BF16)
    vg_spill = dt("vg_spill", [D, T], BF16)
    xbc_spill = dt("xbc_spill", [34 * 128, T], BF16)
    gate_spill = dt("gate_spill", [32 * 128, T], BF16)
    xtm_spill = dt("xtm_spill", [T, DIN], BF16)
    st_spill = dt("st_spill", [NCH * 128, DIN], BF16)
    cc_u_in = dt("cc_u_in", [128, 16], F32)
    cc_u_out = dt("cc_u_out", [8 * 128, 16], F32)
    cc_t_in = dt("cc_t_in", [128, 48], BF16)
    cc_t_out = dt("cc_t_out", [8 * 128, 48], BF16)
    cc_s_in = dt("cc_s_in", [132, DIN], BF16)
    cc_s_out = dt("cc_s_out", [8 * 132, DIN], BF16)
    cc_d_in = dt("cc_d_in", [1, 64], F32)
    cc_d_out = dt("cc_d_out", [8, 64], F32)

    es = contextlib.ExitStack()

    def sbt(name, shape, dtype):
        return es.enter_context(nc.sbuf_tensor(name, shape, dtype))

    STRM = sbt("STRM", [128, NF, T], BF16)
    ARENA = sbt("ARENA", [128, 40960], BF16)
    WL = sbt("WL", [128, 2052], F32)
    WB0 = sbt("WB0", [128, 2048], BF16)
    WB1 = sbt("WB1", [128, 2048], BF16)
    PH = sbt("PH", [128, 2048], F32)
    BCC = sbt("BCC", [128, 2, NCH, CH], BF16)
    DTC = sbt("DTC", [128, NCH, CH], F32)
    SRUN = sbt("SRUN", [128, DIN], BF16)
    SMALL = sbt("SMALL", [128, 1536], F32)
    SMB = sbt("SMB", [128, 2048], BF16)
    HALO = sbt("HALO", [128, 32], F32)
    BIAS = sbt("BIAS", [128, 204], F32)
    IDENT = sbt("IDENT", [128, 128], BF16)
    IDF32 = sbt("IDF32", [128, 128], F32)
    ONESC = sbt("ONESC", [128, 1], BF16)
    ONESR = sbt("ONESR", [1, 128], F32)
    TRIM = sbt("TRIM", [128, 2 * CH], BF16)
    SELM = sbt("SELM", [8, 1024], BF16)
    GV = sbt("GV", [128, 24], F32)
    GR = sbt("GR", [1, 16], F32)
    CCL = sbt("CCL", [128, 48], F32)
    KON = sbt("KON", [128, 2], F32)

    pb = [es.enter_context(nc.psum_tensor(f"pb{i}", [128, 512], F32)) for i in range(8)]

    P = Prog(nc)
    for e in ENGINES:
        P.sems[e] = es.enter_context(nc.semaphore(f"sem_{e}"))
    P.sems["cc"] = es.enter_context(nc.semaphore("sem_cc"))
    for s in Prog.DMA_SEMS:
        P.sems[s] = es.enter_context(nc.semaphore(f"sem_{s}"))
    bk = Banks(P)

    def mm(out_ap, lhsT, rhs, start, stop, inc=0):
        return P.op("pe", lambda e, o=out_ap, l=lhsT, r=rhs, s=start, t=stop:
                    e.matmul(o, l, r, start=s, stop=t), inc=inc)

    def transpose(out_ap, in_ap, inc=0, pdim=128, f32=False):
        idt = IDF32 if f32 else IDENT
        return P.op("pe", lambda e, o=out_ap, i=in_ap, p=pdim, idt=idt: e.transpose(o, i, idt[0:p, 0:p]), inc=inc)

    bank_i = [0]

    def next_bank():
        b = bank_i[0] % 6
        bank_i[0] += 1
        bk.guard(b)
        return b

    wstate = {"i": 0, "use": [None, None], "wl_free": []}

    def load_panel(src2d, kc, queue="sp"):
        i = wstate["i"] % 2
        wstate["i"] += 1
        WB = WB0 if i == 0 else WB1
        P.waits(queue, wstate["wl_free"])
        tl = P.dma(queue, WL[:, 0:kc * 128], src2d.rearrange("(c p) m -> p c m", p=128))
        P.waits("act", [tl, wstate["use"][i]])
        tc = P.op("act", lambda e, WB=WB, n=kc * 128: e.copy(WB[:, 0:n], WL[:, 0:n]), inc=1)
        wstate["wl_free"] = [tc]
        return WB, tc, i

    def panel_used(i, tok):
        wstate["use"][i] = tok

    def ln_tile(t0, tw, fill, out_dma=None, bias_col=None):
        entry_pe = P.now("pe")
        entry_pool = P.now("pool")
        tk = None
        for dM in range(NF):
            bankno, tok = fill(dM)
            b = pb[bankno]
            P.waits("act", [tok] + ([entry_pe] if dM == 0 else []))
            if bias_col is None:
                tc = P.op("act", lambda e, dM=dM, b=b: e.copy(STRM[:, dM, t0:t0 + tw], b[:, 0:tw]), inc=1)
                tq = P.op("act", lambda e, b=b: e.activation(SMB[:, 0:tw], b[:, 0:tw], AF.Square), inc=1)
            else:
                tc = P.op("act", lambda e, dM=dM, b=b: e.activation(
                    STRM[:, dM, t0:t0 + tw], b[:, 0:tw], AF.Identity, bias=bias_col(dM)), inc=1)
                tq = P.op("act", lambda e, dM=dM, b=b: e.activation(
                    SMB[:, 0:tw], b[:, 0:tw], AF.Square, bias=bias_col(dM)), inc=1)
            bk.reader(bankno, tq)
            bk.reader(bankno, tc)
            if dM == 0:
                bk.guard(6)
                bk.guard(7)
            P.waits("pe", [tc, tq])
            mm(pb[6][0:1, 0:tw], ONESC[:, :], STRM[:, dM, t0:t0 + tw], dM == 0, dM == NF - 1)
            tk = mm(pb[7][0:1, 0:tw], ONESC[:, :], SMB[:, 0:tw], dM == 0, dM == NF - 1,
                    inc=(1 if dM == NF - 1 else 0))
        P.waits("dve", [tk])
        P.op("dve", lambda e: e.tensor_scalar(SMALL[0:1, 0:tw], pb[6][0:1, 0:tw], 1.0 / D, None, ALU.mult))
        t1 = P.op("dve", lambda e: e.tensor_scalar(SMALL[0:1, 512:512 + tw], pb[7][0:1, 0:tw], 1.0 / D, None, ALU.mult), inc=1)
        bk.reader(6, t1)
        bk.reader(7, t1)
        P.op("dve", lambda e: e.tensor_tensor(SMALL[0:1, 1024:1024 + tw], SMALL[0:1, 0:tw], SMALL[0:1, 0:tw], ALU.mult))
        t2 = P.op("dve", lambda e: e.tensor_tensor(SMALL[0:1, 512:512 + tw], SMALL[0:1, 512:512 + tw], SMALL[0:1, 1024:1024 + tw], ALU.subtract), inc=1)
        P.waits("act", [t2])
        P.op("act", lambda e: e.activation(SMALL[0:1, 512:512 + tw], SMALL[0:1, 512:512 + tw], AF.Ln, bias=KON[0:1, 0:1]))
        t3 = P.op("act", lambda e: e.activation(SMALL[0:1, 512:512 + tw], SMALL[0:1, 512:512 + tw], AF.Exp, scale=-0.5), inc=1)
        P.waits("dve", [t3])
        t4 = P.op("dve", lambda e: e.tensor_tensor(SMALL[0:1, 1024:1024 + tw], SMALL[0:1, 0:tw], SMALL[0:1, 512:512 + tw], ALU.mult), inc=1)
        bk.guard(6)
        bk.guard(7)
        P.waits("pe", [t4])
        t5 = mm(pb[6][:, 0:tw], ONESR[0:1, :], SMALL[0:1, 512:512 + tw], True, True, inc=1)
        t6 = mm(pb[7][:, 0:tw], ONESR[0:1, :], SMALL[0:1, 1024:1024 + tw], True, True, inc=1)
        P.waits("dve", [t5, t6, entry_pool])
        t7 = P.op("dve", lambda e: e.tensor_copy(SMB[:, 512:512 + tw], pb[6][:, 0:tw]), inc=1)
        t8 = P.op("dve", lambda e: e.tensor_copy(SMB[:, 1024:1024 + tw], pb[7][:, 0:tw]), inc=1)
        bk.reader(6, t7)
        bk.reader(7, t8)
        outs = []
        for dM in range(NF):
            if out_dma is None:
                ta = P.op("dve", lambda e, dM=dM: e.tensor_tensor(
                    STRM[:, dM, t0:t0 + tw], STRM[:, dM, t0:t0 + tw], SMB[:, 512:512 + tw], ALU.mult), inc=1)
                P.waits("pool", [ta, t8])
                tb = P.op("pool", lambda e, dM=dM: e.tensor_tensor(
                    STRM[:, dM, t0:t0 + tw], STRM[:, dM, t0:t0 + tw], SMB[:, 1024:1024 + tw], ALU.subtract), inc=1)
                outs.append(tb)
            else:
                P.waits("dve", [t8] + outs[-1:])
                P.op("dve", lambda e, dM=dM: e.tensor_tensor(
                    SMALL[:, 0:tw], STRM[:, dM, t0:t0 + tw], SMB[:, 512:512 + tw], ALU.mult))
                ta = P.op("dve", lambda e: e.tensor_tensor(
                    SMALL[:, 0:tw], SMALL[:, 0:tw], SMB[:, 1024:1024 + tw], ALU.subtract), inc=1)
                P.waits("sp", [ta])
                outs.append(P.dma("sp", out_dma(dM), SMALL[:, 0:tw], sem="dou"))
        return outs

    def stage(src, dst_fn, rows=128, cols=None):
        tl = P.dma("sp", SMALL[0:rows, 0:cols], src)
        P.waits("dve", [tl])
        tok = P.op("dve", dst_fn, inc=1)
        P.waits("sp", [tok])
        return tok

    P.op("dve", lambda e: e.memset(KON[:, 0:1], EPS))
    P.op("dve", lambda e: e.memset(KON[:, 1:2], 1.0))
    P.op("dve", lambda e: e.memset(ONESC[:, :], 1.0))
    P.op("dve", lambda e: e.memset(ONESR[:, :], 1.0))
    def _ident_copy(e):
        e.tensor_copy(IDENT[:, :], SMALL[:, 0:128])
        return e.tensor_copy(IDF32[:, :], SMALL[:, 0:128])
    stage(identf, _ident_copy, cols=128)
    tkb = stage(bL0a, lambda e: e.tensor_copy(BIAS[:, 0:116], SMALL[:, 0:116]), cols=116)
    P.waits("dve", [tkb])
    P.op("dve", lambda e: e.tensor_scalar(BIAS[:, 116:132], BIAS[:, 80:96], 1.0, None, ALU.add))
    stage(trif, lambda e: e.tensor_copy(TRIM[:, :], SMALL[:, 0:512]), cols=512)
    stage(selmf, lambda e: e.tensor_copy(SELM[:, :], SMALL[0:8, 0:1024]), rows=8, cols=1024)
    t = []
    t.append(P.dma("sp", GV[:, 0:8], mcum, sem="dms"))
    t.append(P.dma("sp", GV[:, 8:16], mtail, sem="dms"))
    t.append(P.dma("sp", GV[:, 16:24], gcolf, sem="dms"))
    t.append(P.dma("sp", GR[0:1, 0:16], grow, sem="dms"))
    P.waits("dve", t)
    tconsts = P.op("dve", lambda e: e.tensor_copy(CCL[:, 0:1], GV[:, 0:1]), inc=1)

    prev = None
    thl = None
    for f in range(NF):
        if prev is not None:
            P.waits("sp", [prev, thl])
        tl = P.dma("sp", WL[:, 0:T + 2], xT[f * 128:(f + 1) * 128, :])
        P.waits("act", [tl])
        prev = P.op("act", lambda e, f=f: e.copy(STRM[:, f, :], WL[:, 1:T + 1]), inc=1)
        P.waits("dve", [tl])
        P.op("dve", lambda e, f=f: e.tensor_copy(HALO[:, 2 * f:2 * f + 1], WL[:, 0:1]))
        thl = P.op("dve", lambda e, f=f: e.tensor_copy(HALO[:, 2 * f + 1:2 * f + 2], WL[:, T + 1:T + 2]), inc=1)
    wstate["wl_free"] = [prev, thl]
    x_ready = [tconsts, prev, thl]

    def dump_sbuf(region_fn, nchunks):
        last = None
        for f in range(nchunks):
            P.waits("dve", [P.now("pool"), P.now("act"), P.now("pe"), last])
            tcv = P.op("dve", lambda e, f=f: e.tensor_copy(PH[:, 0:2048], region_fn(f)), inc=1)
            P.waits("sp", [tcv])
            last = P.dma("sp", out_fm[f * 128:(f + 1) * 128, :], PH[:, 0:2048])
        P.wait("sp", last)
        return True

    def emit_l0a():
        P.waits("pe", x_ready)
        toks_h = []
        for hm in range(4):
            WB, tc, wi = load_panel(wg1T[:, hm * 128:(hm + 1) * 128], 16)
            P.waits("pe", [tc])
            for nt in range(4):
                b = next_bank()
                tpe = None
                for kc in range(16):
                    tpe = mm(pb[b][:, 0:512], WB[:, kc * 128:(kc + 1) * 128],
                             STRM[:, kc, nt * 512:nt * 512 + 512], kc == 0, kc == 15,
                             inc=(1 if kc == 15 else 0))
                panel_used(wi, tpe)
                P.waits("act", [tpe])
                te = P.op("act", lambda e, hm=hm, nt=nt, b=b: e.activation(
                    ARENA[:, hm * 2048 + nt * 512: hm * 2048 + nt * 512 + 512],
                    pb[b][:, 0:512], AF.Relu, bias=BIAS[:, 112 + hm:113 + hm]), inc=1)
                bk.reader(b, te)
                toks_h.append(te)

        if dbg == "h":
            return dump_sbuf(lambda f: ARENA[:, f * 2048:(f + 1) * 2048], 4)
        prev_tp = None
        prev_tvg = None
        prev_tcc = None
        for rm in range(NF):
            WBg, tcg, wig = load_panel(wg2T[:, rm * 128:(rm + 1) * 128], 4)
            P.waits("pe", [tcg] + (toks_h if rm == 0 else []))
            tg = []
            for nt in range(4):
                b = next_bank()
                tpe = None
                for kc in range(4):
                    tpe = mm(pb[b][:, 0:512], WBg[:, kc * 128:(kc + 1) * 128],
                             ARENA[:, kc * 2048 + nt * 512: kc * 2048 + nt * 512 + 512],
                             kc == 0, kc == 3, inc=(1 if kc == 3 else 0))
                panel_used(wig, tpe)
                P.waits("act", [tpe, prev_tvg, prev_tp])
                te = P.op("act", lambda e, nt=nt, rm=rm, b=b: e.activation(
                    SMB[:, nt * 512:nt * 512 + 512], pb[b][:, 0:512], AF.Sigmoid,
                    bias=BIAS[:, 32 + rm:33 + rm]), inc=1)
                bk.reader(b, te)
                tg.append(te)
            WBu, tcu, wiu = load_panel(wuT[:, rm * 128:(rm + 1) * 128], 16)
            P.waits("pe", [tcu])
            tu = []
            for nt in range(4):
                b = next_bank()
                tpe = None
                for kc in range(16):
                    tpe = mm(pb[b][:, 0:512], WBu[:, kc * 128:(kc + 1) * 128],
                             STRM[:, kc, nt * 512:nt * 512 + 512], kc == 0, kc == 15,
                             inc=(1 if kc == 15 else 0))
                panel_used(wiu, tpe)
                P.waits("act", [tpe, prev_tp, prev_tcc])
                te = P.op("act", lambda e, nt=nt, rm=rm, b=b: e.activation(
                    PH[:, nt * 512:nt * 512 + 512], pb[b][:, 0:512], AF.Identity,
                    bias=BIAS[:, rm:rm + 1]), inc=1)
                bk.reader(b, te)
                tu.append(te)
            P.waits("dve", tg + tu + [prev_tp])
            P.op("dve", lambda e: e.tensor_tensor(PH[:, 0:2048], PH[:, 0:2048], SMB[:, 0:2048], ALU.mult))
            tsc = P.op("dve", lambda e: e.tensor_tensor_scan(
                PH[:, 0:2048], PH[:, 0:2048], PH[:, 0:2048], 0.0, ALU.add, ALU.bypass), inc=1)
            P.wait("dve", tsc)
            tcc = P.op("dve", lambda e, rm=rm: e.tensor_copy(CCL[:, rm:rm + 1], PH[:, 2047:2048]), inc=1)
            WBv, tcv, wiv = load_panel(wvT[:, rm * 128:(rm + 1) * 128], 16)
            P.waits("pe", [tcv])
            tv = []
            for nt in range(4):
                b = next_bank()
                tpe = None
                for kc in range(16):
                    tpe = mm(pb[b][:, 0:512], WBv[:, kc * 128:(kc + 1) * 128],
                             STRM[:, kc, nt * 512:nt * 512 + 512], kc == 0, kc == 15,
                             inc=(1 if kc == 15 else 0))
                panel_used(wiv, tpe)
                P.waits("act", [tpe, prev_tvg])
                te = P.op("act", lambda e, nt=nt, rm=rm, b=b: e.activation(
                    ARENA[:, 8192 + nt * 512:8192 + nt * 512 + 512], pb[b][:, 0:512], AF.Identity,
                    bias=BIAS[:, 16 + rm:17 + rm]), inc=1)
                bk.reader(b, te)
                tv.append(te)
            P.waits("dve", tv)
            tvg = P.op("dve", lambda e: e.tensor_tensor(ARENA[:, 8192:10240], ARENA[:, 8192:10240], SMB[:, 0:2048], ALU.mult), inc=1)
            P.waits("pool", [tvg, tsc, prev_tp])
            tp = P.op("pool", lambda e: e.tensor_tensor(ARENA[:, 10240:12288], ARENA[:, 8192:10240], PH[:, 0:2048], ALU.mult), inc=1)
            P.waits("sp", [tvg, tp])
            tdv = P.dma("sp", vg_spill[rm * 128:(rm + 1) * 128, :], ARENA[:, 8192:10240], sem="dv1")
            tdp = P.dma("sp", p_spill[rm * 128:(rm + 1) * 128, :], ARENA[:, 10240:12288], sem="dv2")
            prev_tp = tdp
            prev_tvg = tdv
            prev_tcc = tcc
        if dbg == "tot":
            P.waits("dve", [P.now("pool"), P.now("act"), prev_tp, prev_tvg])
            tcv = P.op("dve", lambda e: e.tensor_copy(SMALL[:, 0:16], CCL[:, 0:16]), inc=1)
            tc2 = P.op("dve", lambda e: e.tensor_copy(PH[:, 0:2048], PH[:, 0:2048]), inc=1)
            P.waits("sp", [tcv, tc2])
            P.dma("sp", out_fm[0:128, 0:16], SMALL[:, 0:16], sem="dou")
            lastd = P.dma("sp", out_fm[128:256, 0:2048], PH[:, 0:2048], sem="dou")
            P.wait("sp", lastd)
            return True
        if dbg in ("vg", "p"):
            sp_spill = vg_spill if dbg == "vg" else p_spill
            last = None
            for f in range(NF):
                P.waits("sp", [P.now("dve"), P.now("pool"), last])
                tl = P.dma("sp", ARENA[:, 0:2048], sp_spill[f * 128:(f + 1) * 128, :], sem="dou")
                P.waits("dve", [tl])
                tcv = P.op("dve", lambda e: e.tensor_copy(PH[:, 0:2048], ARENA[:, 0:2048]), inc=1)
                P.waits("sp", [tcv])
                last = P.dma("sp", out_fm[f * 128:(f + 1) * 128, :], PH[:, 0:2048])
            P.wait("sp", last)
            return True
        P.waits("sp", [P.now("dve")])
        tct = P.dma("sp", cc_u_in.ap(), CCL[:, 0:16], sem="dcc")
        P.waits("pool", [tct])
        P.op("pool", lambda e: e.collective_compute(
            "AllGather", ALU.bypass, replica_groups=[list(range(8))],
            ins=[cc_u_in.ap().opt()], outs=[cc_u_out.ap().opt()]), inc=1, sem="cc")
        cc_tok = ("cc", P.count["cc"])

        pe_done = P.now("pe")
        conv_toks = []
        for f in range(NF):
            P.waits("dve", [pe_done, prev_tp, prev_tvg])
            P.op("dve", lambda e, f=f: e.tensor_scalar(PH[:, 0:T - 1], STRM[:, f, 1:T], BIAS[:, 96 + f:97 + f], None, ALU.mult))
            P.op("dve", lambda e, f=f: e.tensor_scalar(PH[:, T - 1:T], HALO[:, 2 * f + 1:2 * f + 2], BIAS[:, 96 + f:97 + f], None, ALU.mult))
            P.op("dve", lambda e, f=f: e.scalar_tensor_tensor(PH[:, 1:T], STRM[:, f, 0:T - 1], BIAS[:, 64 + f:65 + f], PH[:, 1:T], ALU.mult, ALU.add))
            P.op("dve", lambda e, f=f: e.scalar_tensor_tensor(PH[:, 0:1], HALO[:, 2 * f:2 * f + 1], BIAS[:, 64 + f:65 + f], PH[:, 0:1], ALU.mult, ALU.add))
            P.op("dve", lambda e, f=f: e.scalar_tensor_tensor(PH[:, 0:T], STRM[:, f, :], BIAS[:, 116 + f:117 + f], PH[:, 0:T], ALU.mult, ALU.add))
            t4 = P.op("dve", lambda e, f=f: e.tensor_scalar(PH[:, 0:T], PH[:, 0:T], BIAS[:, 48 + f:49 + f], None, ALU.add), inc=1)
            P.waits("act", [t4])
            t5 = P.op("act", lambda e, f=f: e.copy(STRM[:, f, :], PH[:, 0:T]), inc=1)
            conv_toks.append(t5)
            P.waits("dve", [t5])

        if dbg == "conv":
            return dump_sbuf(lambda f: STRM[:, f, :], NF)
        P.wait("sp", cc_tok)
        P.wait("dve", cc_tok)
        P.op("dve", lambda e: e.memset(CCL[:, 16:32], 0.0))
        tof = None
        for j in range(8):
            tl = P.dma("sp", CCL[:, 32:48], cc_u_out.ap()[j * 128:(j + 1) * 128, :], sem="dg1")
            P.waits("dve", [tl])
            tof = P.op("dve", lambda e, j=j: e.scalar_tensor_tensor(
                CCL[:, 16:32], CCL[:, 32:48], GV[:, j:j + 1], CCL[:, 16:32], ALU.mult, ALU.add), inc=1)
            P.waits("sp", [tof])

        if dbg == "off":
            P.waits("dve", [tof])
            tcv = P.op("dve", lambda e: e.tensor_copy(PH[:, 0:48], CCL[:, 0:48]), inc=1)
            P.waits("sp", [tcv])
            lastd = P.dma("sp", out_fm[0:128, 0:48], PH[:, 0:48], sem="dou")
            P.wait("sp", lastd)
            return True
        P.waits("sp", [prev_tvg, prev_tp])
        P.waits("pool", [prev_tvg, prev_tp])
        acts_done = P.now("act")
        for nt in range(4):
            pe_prev = P.now("pe")
            pptoks = []
            for rk in range(NF):
                sl = rk % 4
                q1 = "sp" if rk % 2 == 0 else "pool"
                q2 = "pool" if rk % 2 == 0 else "sp"
                tlv = P.dma(q1, ARENA[:, 8192 + sl * 512:8192 + sl * 512 + 512],
                            vg_spill[rk * 128:(rk + 1) * 128, nt * 512:nt * 512 + 512], sem=f"ds{sl}")
                tlp = P.dma(q2, ARENA[:, 10240 + sl * 512:10240 + sl * 512 + 512],
                            p_spill[rk * 128:(rk + 1) * 128, nt * 512:nt * 512 + 512], sem=f"dt{sl}")
                P.waits("dve", [tlv, tlp, tof, pe_prev])
                tpq = P.op("dve", lambda e, rk=rk, sl=sl: e.scalar_tensor_tensor(
                    ARENA[:, rk * 512:rk * 512 + 512], ARENA[:, 8192 + sl * 512:8192 + sl * 512 + 512],
                    CCL[:, 16 + rk:17 + rk], ARENA[:, 10240 + sl * 512:10240 + sl * 512 + 512],
                    ALU.mult, ALU.add), inc=1)
                pptoks.append(tpq)
                if rk >= 3:
                    P.waits(q1, [tpq])
                    P.waits(q2, [tpq])
            ppdone = pptoks[-1]

            if dbg == "ppc" and nt == 2:
                P.waits("dve", pptoks)
                lastq = None
                for k in range(4):
                    P.waits("dve", [lastq])
                    tcv = P.op("dve", lambda e, k=k: e.tensor_copy(PH[:, 0:2048], ARENA[:, k * 2048:(k + 1) * 2048]), inc=1)
                    P.waits("sp", [tcv])
                    lastq = P.dma("sp", out_fm[k * 128:(k + 1) * 128, 0:2048], PH[:, 0:2048])
                P.wait("sp", lastq)
                return True

            if dbg == "uv":
                lastd = None
                for dM in range(NF):
                    if dbg2 == "serial":
                        allf = [P.now(e2) for e2 in ("act", "dve", "pool", "sp")]
                        P.waits("pe", allf)
                        P.waits("act", allf)
                        P.waits("dve", allf)
                        P.waits("sp", allf)
                    WBp, tcp, wip = load_panel(wuT[:, dM * 128:(dM + 1) * 128], 16)
                    b = next_bank()
                    P.waits("pe", [tcp, ppdone, acts_done])
                    for rk in range(NF):
                        mm(pb[b][:, 0:512], WBp[:, rk * 128:(rk + 1) * 128],
                           ARENA[:, rk * 512:rk * 512 + 512], rk == 0, False)
                    tpe = mm(pb[b][:, 0:512], IDENT[:, :], STRM[:, dM, nt * 512:nt * 512 + 512],
                             False, True, inc=1)
                    panel_used(wip, tpe)
                    P.waits("act", [tpe, lastd])
                    tev = P.op("act", lambda e, b=b: e.copy(PH[:, 0:512], pb[b][:, 0:512]), inc=1)
                    bk.reader(b, tev)
                    P.waits("sp", [tev])
                    lastd = P.dma("sp", out_fm[dM * 128:(dM + 1) * 128, nt * 512:nt * 512 + 512], PH[:, 0:512])
                if nt == 3:
                    P.wait("sp", lastd)
                    return True
                continue

            def fill(dM, nt=nt, ppdone=ppdone, acts_done=acts_done):
                WBp, tcp, wip = load_panel(wuT[:, dM * 128:(dM + 1) * 128], 16)
                b = next_bank()
                P.waits("pe", [tcp, ppdone, acts_done])
                for rk in range(NF):
                    mm(pb[b][:, 0:512], WBp[:, rk * 128:(rk + 1) * 128],
                       ARENA[:, rk * 512:rk * 512 + 512], rk == 0, False)
                tpe = mm(pb[b][:, 0:512], IDENT[:, :], STRM[:, dM, nt * 512:nt * 512 + 512],
                         False, True, inc=1)
                panel_used(wip, tpe)
                return b, tpe

            ln_tile(nt * 512, 512, fill)
        return False


    def emit_moe(gw_ap, b1_parts, b1_cols, lnb_src, lnb_cols, b2_ap, experts, nh, w2kc,
                 final=False):
        P.phase_fence()
        BCCF = BCC.ap().rearrange("p a b c -> p (a b c)")
        GFM = BCCF[0:8, 0:2048]
        B2B = BCCF[0:8, 2048:4096]
        prev_done = [P.now("pool"), P.now("act"), P.now("dve"), P.now("pe")]
        P.waits("sp", prev_done)
        off = 0
        tbs = []
        for bsrc, bw in b1_parts:
            tbs.append(P.dma("sp", SMALL[:, off:off + bw], bsrc, sem="dms"))
            off += bw
        assert off == b1_cols
        tb2 = P.dma("sp", SMALL[:, b1_cols:b1_cols + lnb_cols], lnb_src, sem="dms")
        P.waits("dve", tbs + [tb2] + prev_done)
        tbias = P.op("dve", lambda e: e.tensor_copy(BIAS[:, 0:b1_cols + lnb_cols], SMALL[:, 0:b1_cols + lnb_cols]), inc=1)
        P.waits("sp", [tbias])
        tb3 = P.dma("sp", WL[0:8, 0:2048], b2_ap, sem="dms")
        P.waits("dve", [tb3])
        tb2b = P.op("dve", lambda e: e.tensor_copy(B2B, WL[0:8, 0:2048]), inc=1)
        P.waits("sp", [tb2b])
        tg0 = P.dma("sp", WL[:, 0:128], gw_ap.rearrange("(c p) m -> p c m", p=128), sem="dms")
        P.waits("dve", [tg0])
        tgw = P.op("dve", lambda e: e.tensor_copy(SMB[:, 1536:1664], WL[:, 0:128]), inc=1)
        P.waits("sp", [tgw])
        wstate["wl_free"] = [tgw]

        for nt in range(4):
            b = next_bank()
            P.waits("pe", [tgw, tbias] + (prev_done if nt == 0 else []))
            tpe = None
            for kc in range(16):
                tpe = mm(pb[b][0:8, 0:512], SMB[:, 1536 + kc * 8:1536 + kc * 8 + 8],
                         STRM[:, kc, nt * 512:nt * 512 + 512], kc == 0, kc == 15,
                         inc=(1 if kc == 15 else 0))
            P.waits("act", [tpe])
            tev = P.op("act", lambda e, b=b: e.copy(SMALL[0:8, 384:896], pb[b][0:8, 0:512]), inc=1)
            bk.reader(b, tev)
            b2_ = next_bank()
            P.waits("pe", [tev])
            ttr = None
            for sc in range(4):
                ttr = transpose(pb[b2_][:, sc * 8:sc * 8 + 8], SMALL[0:8, 384 + sc * 128:384 + sc * 128 + 128],
                                inc=(1 if sc == 3 else 0), pdim=8, f32=True)
            P.waits("dve", [ttr])
            tlg = P.op("dve", lambda e, b2_=b2_: e.tensor_copy(SMALL[:, 0:32], pb[b2_][:, 0:32]), inc=1)
            bk.reader(b2_, tlg)
            for sc in range(4):
                lg = SMALL[:, sc * 8:sc * 8 + 8]
                m1 = SMALL[:, 128 + sc:129 + sc]
                m2 = SMALL[:, 132 + sc:133 + sc]
                dd = SMALL[:, 136 + sc:137 + sc]
                eq1 = SMALL[:, 140 + sc * 8:148 + sc * 8]
                lm = SMALL[:, 172 + sc * 8:180 + sc * 8]
                oth = SMALL[:, 204 + sc * 8:212 + sc * 8]
                P.op("dve", lambda e, lg=lg, m1=m1: e.reduce_max(m1, lg, axis=mybir.AxisListType.X))
                P.op("dve", lambda e, lg=lg, m1=m1, eq1=eq1: e.tensor_scalar(eq1, lg, m1, None, ALU.is_equal))
                P.op("dve", lambda e, eq1=eq1, lg=lg, lm=lm: e.scalar_tensor_tensor(lm, eq1, -1e30, lg, ALU.mult, ALU.add))
                P.op("dve", lambda e, lm=lm, m2=m2: e.reduce_max(m2, lm, axis=mybir.AxisListType.X))
                P.op("dve", lambda e, lm=lm, m2=m2: e.tensor_scalar(SMALL[:, 236:244], lm, m2, None, ALU.is_equal))
                P.op("dve", lambda e, m2=m2, m1=m1, dd=dd: e.tensor_tensor(dd, m2, m1, ALU.subtract))
                P.op("dve", lambda e, eq1=eq1, dd=dd, m1=m1, oth=oth: e.tensor_scalar(oth, eq1, dd, m1, ALU.mult, ALU.add))
                P.op("dve", lambda e, lg=lg, oth=oth: e.tensor_tensor(SMALL[:, 244:252], lg, oth, ALU.subtract))
                P.op("dve", lambda e, eq1=eq1, oth=oth: e.tensor_tensor(oth, eq1, SMALL[:, 236:244], ALU.add))
                tsg = P.op("dve", lambda e, sc=sc: e.tensor_copy(SMALL[:, 268 + sc * 8:276 + sc * 8], SMALL[:, 244:252]), inc=1)
            P.waits("act", [tsg])
            tsig = P.op("act", lambda e: e.activation(SMALL[:, 268:300], SMALL[:, 268:300], AF.Sigmoid), inc=1)
            P.waits("dve", [tsig])
            tgt = P.op("dve", lambda e: e.tensor_tensor(SMB[:, 512:544], SMALL[:, 268:300], SMALL[:, 204:236], ALU.mult), inc=1)
            b3_ = next_bank()
            pbf3 = pb[b3_].ap().bitcast(BF16)
            P.waits("pe", [tgt])
            ttb = None
            for sc in range(4):
                ttb = transpose(pbf3[0:8, sc * 128:sc * 128 + 128], SMB[:, 512 + sc * 8:520 + sc * 8],
                                inc=(1 if sc == 3 else 0))
            P.waits("dve", [ttb])
            tgf = P.op("dve", lambda e, pbf3=pbf3, nt=nt: e.tensor_copy(GFM[:, nt * 512:nt * 512 + 512], pbf3[0:8, 0:512]), inc=1)
            bk.reader(b3_, tgf)

        gfm_done = P.now("dve")
        nexp = len(experts)

        all_outs = []
        for nt in range(4):
            pe_prev = P.now("pe")
            harena_toks = []
            for e_, (w1_src, b1col, w2_src, gated) in enumerate(experts):
                for hm in range(nh):
                    WB, tc, wi = load_panel(w1_src(hm), 16)
                    b = next_bank()
                    P.waits("pe", [tc] + ([gfm_done, pe_prev] if (e_ == 0 and hm == 0) else []))
                    tpe = None
                    for kc in range(16):
                        tpe = mm(pb[b][:, 0:512], WB[:, kc * 128:(kc + 1) * 128],
                                 STRM[:, kc, nt * 512:nt * 512 + 512], kc == 0, kc == 15,
                                 inc=(1 if kc == 15 else 0))
                    panel_used(wi, tpe)
                    P.waits("act", [tpe])
                    te = P.op("act", lambda e, e_=e_, hm=hm, b=b, bcol=b1col(hm): e.activation(
                        ARENA[:, e_ * nh * 512 + hm * 512: e_ * nh * 512 + hm * 512 + 512],
                        pb[b][:, 0:512], AF.Gelu, bias=bcol), inc=1)
                    bk.reader(b, te)
                    harena_toks.append(te)
                if gated:
                    bg_ = next_bank()
                    P.waits("pe", harena_toks[-nh:])
                    tge = mm(pb[bg_][:, 0:512], SELM[:, e_ * 128:(e_ + 1) * 128],
                             GFM[:, nt * 512:nt * 512 + 512], True, True, inc=1)
                    P.waits("dve", [tge])
                    tgev = P.op("dve", lambda e, bg_=bg_: e.tensor_copy(SMB[:, 0:512], pb[bg_][:, 0:512]), inc=1)
                    bk.reader(bg_, tgev)
                    tsc_ = None
                    for hm in range(nh):
                        tsc_ = P.op("dve", lambda e, e_=e_, hm=hm: e.tensor_tensor(
                            ARENA[:, e_ * nh * 512 + hm * 512: e_ * nh * 512 + hm * 512 + 512],
                            ARENA[:, e_ * nh * 512 + hm * 512: e_ * nh * 512 + hm * 512 + 512],
                            SMB[:, 0:512], ALU.mult), inc=(1 if hm == nh - 1 else 0))
                    harena_toks.append(tsc_)
            hdone = [P.now("dve"), P.now("act")]

            def fill(dM, nt=nt, hdone=hdone):
                b = next_bank()
                first = True
                for e_, (w1_src, b1col, w2_src, gated) in enumerate(experts):
                    WB, tc, wi = load_panel(w2_src(dM), w2kc)
                    P.waits("pe", [tc] + (hdone if first else []))
                    tpe = None
                    for kc in range(w2kc):
                        tpe = mm(pb[b][:, 0:512], WB[:, kc * 128:(kc + 1) * 128],
                                 ARENA[:, e_ * nh * 512 + kc * 512: e_ * nh * 512 + kc * 512 + 512],
                                 first, False, inc=(1 if kc == w2kc - 1 else 0))
                        first = False
                    panel_used(wi, tpe)
                mm(pb[b][:, 0:512], B2B[:, dM * 128:(dM + 1) * 128], GFM[:, nt * 512:nt * 512 + 512], False, False)
                tpe = mm(pb[b][:, 0:512], IDENT[:, :], STRM[:, dM, nt * 512:nt * 512 + 512], False, True, inc=1)
                return b, tpe

            outs = ln_tile(nt * 512, 512, fill,
                           bias_col=lambda dM: BIAS[:, b1_cols + dM:b1_cols + dM + 1],
                           out_dma=(lambda dM, nt=nt: out_fm[dM * 128:(dM + 1) * 128, nt * 512:nt * 512 + 512]) if final else None)
            all_outs += outs
        if final:
            for o in all_outs:
                P.wait("sp", o)
            return True
        return False

    def emit_l0f():
        experts = []
        for e_ in range(8):
            experts.append((lambda hm, e_=e_: w10[e_][:, hm * 128:(hm + 1) * 128],
                            lambda hm, e_=e_: BIAS[:, e_ * 8 + hm:e_ * 8 + hm + 1],
                            lambda dM, e_=e_: w20[e_][:, dM * 128:(dM + 1) * 128], True))
        experts.append((lambda hm: sw1[:, hm * 128:(hm + 1) * 128],
                        lambda hm: BIAS[:, 64 + hm:65 + hm],
                        lambda dM: sw2[0:1024, dM * 128:(dM + 1) * 128], False))
        experts.append((lambda hm: sw1[:, 1024 + hm * 128:1024 + hm * 128 + 128],
                        lambda hm: BIAS[:, 72 + hm:73 + hm],
                        lambda dM: sw2[1024:2048, dM * 128:(dM + 1) * 128], False))
        return emit_moe(gw0, [(b10, 80)], 80, sb2c, 16, b20, experts, nh=8, w2kc=8, final=False)

    def emit_l1f(final=True):
        experts = []
        for e_ in range(8):
            experts.append((lambda hm, e_=e_: wdT[e_][:, hm * 128:(hm + 1) * 128],
                            lambda hm, e_=e_: BIAS[:, e_ * 4 + hm:e_ * 4 + hm + 1],
                            lambda dM, e_=e_: wuT1[e_][:, dM * 128:(dM + 1) * 128], True))
        experts.append((lambda hm: wsdT[:, hm * 128:(hm + 1) * 128],
                        lambda hm: BIAS[:, 32 + hm:33 + hm],
                        lambda dM: wsuT[:, dM * 128:(dM + 1) * 128], False))
        return emit_moe(gw1, [(bd1, 32), (bsdc, 4)], 36, bsuc, 16, bu1, experts, nh=4, w2kc=4, final=final)


    def emit_l1a():
        P.phase_fence()
        prev_done = [P.now("pool"), P.now("act"), P.now("dve"), P.now("pe")]
        P.waits("sp", prev_done)
        t1 = P.dma("sp", WL[0:64, 0:2048], expmf[:, 0:2048], sem="dms")
        P.waits("dve", [t1] + prev_done)
        te1 = P.op("dve", lambda e: e.tensor_copy(ARENA[0:64, 33920:35968], WL[0:64, 0:2048]), inc=1)
        P.waits("sp", [te1])
        t2 = P.dma("sp", WL[0:64, 0:2048], expmf[:, 2048:4096], sem="dms")
        P.waits("dve", [t2])
        te2 = P.op("dve", lambda e: e.tensor_copy(ARENA[0:64, 35968:38016], WL[0:64, 0:2048]), inc=1)
        P.waits("sp", [te2])
        t3 = P.dma("sp", SMALL[:, 0:136], cw1, sem="dms")
        t4 = P.dma("sp", SMALL[:, 136:170], cb1, sem="dms")
        t5 = P.dma("sp", SMALL[0:64, 170:171], dtb, sem="dms")
        t6 = P.dma("sp", SMALL[0:64, 171:172], alog, sem="dms")
        t7 = P.dma("sp", SMALL[:, 172:204], nwc, sem="dms")
        P.waits("dve", [t3, t4, t5, t6, t7])
        tb = P.op("dve", lambda e: e.tensor_copy(BIAS[:, 0:204], SMALL[:, 0:204]), inc=1)
        P.waits("act", [tb])
        ta = P.op("act", lambda e: e.activation(BIAS[0:64, 171:172], BIAS[0:64, 171:172], AF.Exp), inc=1)
        P.waits("dve", [ta])
        tneg = P.op("dve", lambda e: e.tensor_scalar(BIAS[0:64, 171:172], BIAS[0:64, 171:172], -1.0, None, ALU.mult), inc=1)
        P.waits("sp", [tb])
        wstate["wl_free"] = [te2]

        tt_in = P.dma("sp", cc_t_in.ap(), STRM[:, :, T - 3:T], sem="dcc")
        P.waits("pool", [tt_in])
        P.op("pool", lambda e: e.collective_compute(
            "AllGather", ALU.bypass, replica_groups=[list(range(8))],
            ins=[cc_t_in.ap().opt()], outs=[cc_t_out.ap().opt()]), inc=1, sem="cc")
        cct = ("cc", P.count["cc"])
        P.wait("sp", cct)
        P.op("dve", lambda e: e.memset(SMB[:, 0:48], 0.0))
        ttl = None
        for j in range(8):
            tl = P.dma("sp", SMB[:, 1664:1712], cc_t_out.ap()[j * 128:(j + 1) * 128, :], sem="dtl")
            P.waits("dve", [tl])
            ttl = P.op("dve", lambda e, j=j: e.scalar_tensor_tensor(
                SMB[:, 0:48], SMB[:, 1664:1712], GV[:, 8 + j:9 + j], SMB[:, 0:48], ALU.mult, ALU.add), inc=1)
            P.waits("sp", [ttl])

        stage_tok = [None, None]
        dma_tok = [None, None]
        dma_tok2 = [None, None]
        conv_ph_free = [None]
        for mi in range(67):
            WB, tc, wi = load_panel(ipT[:, mi * 128:(mi + 1) * 128], 16)
            bset = (mi % 2) * 4
            for bb in range(4):
                bk.guard(bset + bb)
            P.waits("pe", [tc, ttl] if mi == 0 else [tc])
            tpe = None
            for ng in range(4):
                for kc in range(16):
                    tpe = mm(pb[bset + ng][:, 0:512], WB[:, kc * 128:(kc + 1) * 128],
                             STRM[:, kc, ng * 512:ng * 512 + 512], kc == 0, kc == 15,
                             inc=(1 if (ng == 3 and kc == 15) else 0))
            panel_used(wi, tpe)
            sslot = mi % 2
            P.waits("act", [tpe, dma_tok[sslot]])
            if mi < 32:
                tevs = []
                for ng in range(4):
                    tev = P.op("act", lambda e, ng=ng, bset=bset, sslot=sslot: e.activation(
                        ARENA[:, sslot * 2048 + ng * 512: sslot * 2048 + ng * 512 + 512],
                        pb[bset + ng][:, 0:512], AF.Silu), inc=1)
                    bk.reader(bset + ng, tev)
                    tevs.append(tev)
                P.waits("sp", tevs)
                dma_tok[sslot] = P.dma("sp", gate_spill[mi * 128:(mi + 1) * 128, :], ARENA[:, sslot * 2048:sslot * 2048 + 2048], sem=f"dp{sslot}")
            elif mi < 66:
                f = mi - 32
                tevs = []
                for ng in range(4):
                    tev = P.op("act", lambda e, ng=ng, bset=bset, sslot=sslot: e.copy(
                        ARENA[:, sslot * 2048 + ng * 512: sslot * 2048 + ng * 512 + 512],
                        pb[bset + ng][:, 0:512]), inc=1)
                    bk.reader(bset + ng, tev)
                    tevs.append(tev)
                bk.guard(bset + 0)
                ttp = None
                for kc in range(16):
                    ttp = mm(pb[bset + 0][:, 0:3], WB[:, kc * 128:(kc + 1) * 128],
                             SMB[:, kc * 3:kc * 3 + 3], kc == 0, kc == 15,
                             inc=(1 if kc == 15 else 0))
                panel_used(wi, ttp)
                P.waits("dve", [ttp])
                tt3 = P.op("dve", lambda e, f=f, bset=bset: e.tensor_copy(
                    SMB[:, 48 + f * 3:48 + f * 3 + 3], pb[bset + 0][:, 0:3]), inc=1)
                bk.reader(bset + 0, tt3)
                STG = ARENA[:, sslot * 2048:sslot * 2048 + 2048]
                T3 = SMB[:, 48 + f * 3:48 + f * 3 + 3]
                wc = lambda k, f=f: BIAS[:, f * 4 + k:f * 4 + k + 1]
                P.waits("dve", tevs + [conv_ph_free[0]])
                P.op("dve", lambda e, STG=STG, wc=wc: e.tensor_scalar(PH[:, 3:2048], STG[:, 0:2045], wc(0), None, ALU.mult))
                P.op("dve", lambda e, T3=T3, wc=wc: e.tensor_scalar(PH[:, 0:3], T3[:, 0:3], wc(0), None, ALU.mult))
                P.op("dve", lambda e, STG=STG, wc=wc: e.scalar_tensor_tensor(PH[:, 2:2048], STG[:, 0:2046], wc(1), PH[:, 2:2048], ALU.mult, ALU.add))
                P.op("dve", lambda e, T3=T3, wc=wc: e.scalar_tensor_tensor(PH[:, 0:2], T3[:, 1:3], wc(1), PH[:, 0:2], ALU.mult, ALU.add))
                P.op("dve", lambda e, STG=STG, wc=wc: e.scalar_tensor_tensor(PH[:, 1:2048], STG[:, 0:2047], wc(2), PH[:, 1:2048], ALU.mult, ALU.add))
                P.op("dve", lambda e, T3=T3, wc=wc: e.scalar_tensor_tensor(PH[:, 0:1], T3[:, 2:3], wc(2), PH[:, 0:1], ALU.mult, ALU.add))
                tcv4 = P.op("dve", lambda e, STG=STG, wc=wc: e.scalar_tensor_tensor(PH[:, 0:2048], STG[:, 0:2048], wc(3), PH[:, 0:2048], ALU.mult, ALU.add), inc=1)
                P.waits("act", [tcv4, dma_tok2[sslot]])
                tsl = P.op("act", lambda e, f=f, sslot=sslot: e.activation(
                    ARENA[:, 4096 + sslot * 2048:4096 + sslot * 2048 + 2048], PH[:, 0:2048], AF.Silu,
                    bias=BIAS[:, 136 + f:137 + f]), inc=1)
                conv_ph_free[0] = tsl
                P.waits("sp", [tsl])
                dma_tok2[sslot] = P.dma("sp", xbc_spill[f * 128:(f + 1) * 128, :], ARENA[:, 4096 + sslot * 2048:4096 + sslot * 2048 + 2048], sem=f"dp{sslot}")
                dma_tok[sslot] = dma_tok2[sslot]
            else:
                P.waits("act", [conv_ph_free[0]])
                tevs = []
                for ng in range(4):
                    tev = P.op("act", lambda e, ng=ng, bset=bset: e.copy(
                        PH[0:64, ng * 512:ng * 512 + 512], pb[bset + ng][0:64, 0:512]), inc=1)
                    bk.reader(bset + ng, tev)
                    tevs.append(tev)
                dt_done = tevs[-1]

        if dbg == "p1a":
            last = None
            for f in range(NF):
                P.waits("sp", [P.now("dve"), P.now("act"), last])
                srcsp = gate_spill if f < 8 else xbc_spill
                roff = f * 128 if f < 8 else (f - 8) * 128
                tl = P.dma("sp", ARENA[:, 0:2048], srcsp[roff:roff + 128, :], sem="dou")
                P.waits("dve", [tl])
                tcv = P.op("dve", lambda e: e.tensor_copy(PH[:, 0:2048], ARENA[:, 0:2048]), inc=1)
                P.waits("sp", [tcv])
                last = P.dma("sp", out_fm[f * 128:(f + 1) * 128, :], PH[:, 0:2048])
                P.waits("dve", [last])
            P.wait("sp", last)
            return True
        P.waits("dve", [dt_done, tneg])
        P.op("dve", lambda e: e.tensor_scalar(PH[0:64, 0:2048], PH[0:64, 0:2048], BIAS[0:64, 170:171], None, ALU.add))
        DTCF = DTC.ap().rearrange("p a b -> p (a b)")
        P.op("dve", lambda e: e.tensor_scalar(DTCF[0:64, 0:2048], PH[0:64, 0:2048], -1.0, None, ALU.mult))
        tz = P.op("dve", lambda e: e.tensor_tensor(DTCF[0:64, 0:2048], DTCF[0:64, 0:2048], PH[0:64, 0:2048], ALU.max), inc=1)
        P.waits("act", [tz])
        P.op("act", lambda e: e.activation(DTCF[0:64, 0:2048], DTCF[0:64, 0:2048], AF.Exp, scale=-1.0))
        tln = P.op("act", lambda e: e.activation(DTCF[0:64, 0:2048], DTCF[0:64, 0:2048], AF.Ln, bias=KON[0:64, 1:2]), inc=1)
        P.waits("dve", [tln])
        P.op("dve", lambda e: e.tensor_scalar(PH[0:64, 0:2048], PH[0:64, 0:2048], 0.0, None, ALU.max))
        tdt = P.op("dve", lambda e: e.tensor_tensor(PH[0:64, 0:2048], PH[0:64, 0:2048], DTCF[0:64, 0:2048], ALU.add), inc=1)
        tdtb = P.op("dve", lambda e: e.tensor_copy(ARENA[0:64, 38528:40576], PH[0:64, 0:2048]), inc=1)
        tda = P.op("dve", lambda e: e.tensor_scalar(DTCF[0:64, 0:2048], PH[0:64, 0:2048], BIAS[0:64, 171:172], None, ALU.mult), inc=1)
        tscn = None
        for c in range(NCH):
            tscn = P.op("dve", lambda e, c=c: e.tensor_tensor_scan(
                DTC[0:64, c, 0:256], DTC[0:64, c, 0:256], DTC[0:64, c, 0:256], 0.0, ALU.add, ALU.bypass), inc=1)
        P.wait("dve", tscn)
        P.op("dve", lambda e: e.memset(SMALL[0:64, 172:173], 0.0))
        tds = None
        for c in range(NCH):
            tds = P.op("dve", lambda e, c=c: e.tensor_tensor(
                SMALL[0:64, 172:173], SMALL[0:64, 172:173], DTC[0:64, c, 255:256], ALU.add), inc=(1 if c == NCH - 1 else 0))
        if dbg == "dtc":
            P.waits("dve", [tds, tdtb])
            tcd = P.op("dve", lambda e: e.tensor_copy(PH[64:128, 0:2048] if False else SMALL[0:64, 0:1024], SMALL[0:64, 0:1024]), inc=1)
            tdd2 = P.op("dve", lambda e: e.tensor_copy(DTCF[64:128, 0:2048] if False else PH[0:64, 0:2048], PH[0:64, 0:2048]), inc=1)
            P.waits("sp", [tcd, tdd2])
            l1 = P.dma("sp", out_fm[0:64, 0:2048], DTCF[0:64, 0:2048], sem="dou")
            l2 = P.dma("sp", out_fm[64:128, 0:2048], PH[0:64, 0:2048], sem="dou")
            P.wait("sp", l2)
            P.wait("sp", l1)
            return True

        P.phase_fence()
        XSo = 0
        A0 = 0
        XTo = 17536
        XDo = 25728
        EXo = 33920
        BTo = 38272
        P.op("dve", lambda e: e.memset(SRUN[:, :], 0.0))
        prev_chunk_toks = []
        prev_tspx = None
        for c in range(NCH):
            if c == 0:
                P.waits("sp", [P.now("pe"), P.now("dve"), P.now("act"), P.now("pool")])
                P.dma_fence("sp")
            else:
                P.waits("sp", prev_chunk_toks)
            tl = P.dma("sp", ARENA[:, XSo:XSo + 8704].rearrange("p (f t) -> p f t", t=256),
                       xbc_spill[:, c * 256:(c + 1) * 256].rearrange("(f p) t -> p f t", p=128), sem="dxb")
            xs_toks = [tl]
            P.waits("dve", [tl])
            tbc = P.op("dve", lambda e, c=c: e.tensor_copy(BCC[:, 0, c, :], ARENA[:, XSo + 32 * 256:XSo + 32 * 256 + 256]), inc=1)
            tcc2 = P.op("dve", lambda e, c=c: e.tensor_copy(BCC[:, 1, c, :], ARENA[:, XSo + 33 * 256:XSo + 33 * 256 + 256]), inc=1)
            P.waits("pe", xs_toks)
            xtev = []
            for half in range(8):
                tc_ = half // 4
                fg = half % 4
                bno = half % 2
                bk.guard(bno)
                pbf = pb[bno].ap().bitcast(BF16)
                ttp = None
                for k in range(8):
                    f = fg * 8 + k
                    ttp = transpose(pbf[:, k * 128:k * 128 + 128],
                                    ARENA[:, XSo + f * 256 + tc_ * 128: XSo + f * 256 + tc_ * 128 + 128],
                                    inc=(1 if k == 7 else 0))
                P.waits("dve", [ttp] + ([prev_tspx] if half == 0 else []))
                tev = P.op("dve", lambda e, pbf=pbf, tc_=tc_, fg=fg: e.tensor_copy(
                    ARENA[:, XTo + tc_ * 4096 + fg * 1024: XTo + tc_ * 4096 + fg * 1024 + 1024],
                    pbf[:, 0:1024]), inc=1)
                bk.reader(bno, tev)
                xtev.append(tev)
            P.waits("sp", xtev)
            tspx = P.dma("sp", xtm_spill[c * 256:(c + 1) * 256, :].rearrange("(tc p) f -> p tc f", p=128),
                         ARENA[:, XTo:XTo + 8192].rearrange("p (tc f) -> p tc f", f=4096), sem="dxt")
            prev_tspx = tspx
            P.waits("act", [P.now("dve")])
            tdec = P.op("act", lambda e, c=c: e.activation(SMALL[0:64, 768:1024], DTC[0:64, c, :], AF.Exp,
                                                           scale=-1.0, bias=DTC[0:64, c, 255:256]), inc=1)
            tchk = P.op("act", lambda e, c=c: e.activation(SMALL[0:64, 173:174], DTC[0:64, c, 255:256], AF.Exp), inc=1)
            P.waits("dve", [tdec, tchk])
            P.op("dve", lambda e, c=c: e.tensor_tensor(SMALL[0:64, 1024:1280], SMALL[0:64, 768:1024], ARENA[0:64, 38528 + c * 256:38528 + c * 256 + 256], ALU.mult))
            tdd = P.op("dve", lambda e: e.tensor_copy(SMB[0:64, 0:256], SMALL[0:64, 1024:1280]), inc=1)
            bk.guard(2)
            pbf2 = pb[2].ap().bitcast(BF16)
            P.waits("pe", [tbc])
            tbt = None
            for tc_ in range(2):
                tbt = transpose(pbf2[:, tc_ * 128:tc_ * 128 + 128], BCC[:, 0, c, tc_ * 128:tc_ * 128 + 128],
                                inc=(1 if tc_ == 1 else 0))
            P.waits("dve", [tbt])
            tbte = P.op("dve", lambda e: e.tensor_copy(ARENA[:, BTo:BTo + 256], pbf2[:, 0:256]), inc=1)
            bk.reader(2, tbte)
            xde = []
            for tc_ in range(2):
                for sl in range(8):
                    bno = 3 + (sl % 2)
                    bk.guard(bno)
                    P.waits("pe", [tdd])
                    tex = mm(pb[bno][:, 0:512], SMB[0:64, tc_ * 128:tc_ * 128 + 128],
                             ARENA[0:64, EXo + sl * 512:EXo + sl * 512 + 512], True, True, inc=1)
                    P.waits("dve", [tex, tspx])
                    txd = P.op("dve", lambda e, tc_=tc_, sl=sl, bno=bno: e.tensor_tensor(
                        ARENA[:, XDo + tc_ * 4096 + sl * 512: XDo + tc_ * 4096 + sl * 512 + 512],
                        ARENA[:, XTo + tc_ * 4096 + sl * 512: XTo + tc_ * 4096 + sl * 512 + 512],
                        pb[bno][:, 0:512], ALU.mult), inc=1)
                    bk.reader(bno, txd)
                    xde.append(txd)
            P.waits("sp", [P.now("dve")])
            tsps = P.dma("sp", st_spill[c * 128:(c + 1) * 128, :], SRUN[:, :], sem="dss")
            P.waits("dve", [tchk])
            tac = P.op("dve", lambda e: e.tensor_copy(SMB[0:64, 256:257], SMALL[0:64, 173:174]), inc=1)
            prev_chunk_toks = []
            for g in range(8):
                bs = 5
                bk.guard(bs)
                P.waits("pe", xde + [tbte])
                tst = None
                for h8 in range(8):
                    h = g * 8 + h8
                    for tc_ in range(2):
                        tst = mm(pb[bs][:, h8 * 64:h8 * 64 + 64], ARENA[:, BTo + tc_ * 128:BTo + tc_ * 128 + 128],
                                 ARENA[:, XDo + tc_ * 4096 + h * 64: XDo + tc_ * 4096 + h * 64 + 64],
                                 tc_ == 0, tc_ == 1, inc=(1 if (h8 == 7 and tc_ == 1) else 0))
                bk.guard(6)
                P.waits("pe", [tac])
                tar = mm(pb[6][0:1, 0:512], SMB[0:64, 256:257], ARENA[0:64, EXo + g * 512:EXo + g * 512 + 512],
                         True, True, inc=1)
                P.waits("dve", [tar])
                trow = P.op("dve", lambda e: e.tensor_copy(SMALL[0:1, 512:1024], pb[6][0:1, 0:512]), inc=1)
                bk.reader(6, trow)
                bk.guard(7)
                P.waits("pe", [trow])
                tax = mm(pb[7][:, 0:512], ONESR[0:1, :], SMALL[0:1, 512:1024], True, True, inc=1)
                P.waits("dve", [tax, tst, tsps])
                t1_ = P.op("dve", lambda e, g=g: e.tensor_tensor(
                    SRUN[:, g * 512:g * 512 + 512], SRUN[:, g * 512:g * 512 + 512], pb[7][:, 0:512], ALU.mult), inc=1)
                bk.reader(7, t1_)
                t2_ = P.op("dve", lambda e, g=g, bs=bs: e.tensor_tensor(
                    SRUN[:, g * 512:g * 512 + 512], SRUN[:, g * 512:g * 512 + 512], pb[bs][:, 0:512], ALU.add), inc=1)
                bk.reader(bs, t2_)
                prev_chunk_toks.append(t2_)
            prev_chunk_toks.append(tsps)

        P.phase_fence()
        tex = P.op("act", lambda e: e.activation(SMALL[0:64, 174:175], SMALL[0:64, 172:173], AF.Exp), inc=1)
        bk.guard(3)
        P.waits("pe", [tex])
        ttr = transpose(pb[3][0:1, 0:64], SMALL[0:64, 174:175], inc=1, pdim=64, f32=True)
        P.waits("dve", [ttr])
        trw = P.op("dve", lambda e: e.tensor_copy(SMALL[0:1, 175:239], pb[3][0:1, 0:64]), inc=1)
        bk.reader(3, trw)
        P.waits("dve", [trw])
        tdb = P.op("dve", lambda e: e.tensor_copy(SMB[0:1, 258:322], SMALL[0:1, 175:239]), inc=1)
        tz1 = P.op("dve", lambda e: e.memset(ARENA[0:4, A0:A0 + 4096], 0.0), inc=1)
        P.waits("sp", [tz1])
        tz2 = P.dma("sp", cc_s_in.ap()[128:132, :], ARENA[0:4, A0:A0 + 4096], sem="dcc")
        P.waits("sp", [tdb, tz2, P.now("dve")])
        td1 = P.dma("sp", cc_s_in.ap()[128:129, 0:64], SMB[0:1, 258:322], sem="dcc")
        td2 = P.dma("sp", cc_s_in.ap()[0:128, :], SRUN[:, :], sem="dcc")
        P.waits("pool", [td1, td2])
        P.op("pool", lambda e: e.collective_compute(
            "AllGather", ALU.bypass, replica_groups=[list(range(8))],
            ins=[cc_s_in.ap().opt()],
            outs=[cc_s_out.ap().opt()]), inc=1, sem="cc")
        ccs = ("cc", P.count["cc"])
        P.wait("sp", ccs)
        P.wait("dve", ccs)
        tdr0 = P.dma("sp", SMB[0:1, 1024:1536].rearrange("p (s h) -> p s h", h=64),
                     cc_s_out.ap().rearrange("(s p) h -> s p h", p=132)[:, 128, 0:64], sem="dcc")
        P.waits("dve", [tdr0])
        tdr = P.op("dve", lambda e: e.tensor_copy(SMALL[0:1, 512:1024], SMB[0:1, 1024:1536]), inc=1)
        tmz = P.op("dve", lambda e: e.memset(SRUN[:, :], 0.0), inc=1)
        for j in range(8):
            tfl = P.dma("sp", ARENA[:, A0:A0 + 4096], cc_s_out.ap()[j * 132:j * 132 + 128, :], sem="dfl")
            P.waits("dve", [tdr, tfl, tmz])
            tara = P.op("dve", lambda e, j=j: e.tensor_scalar(
                SMALL[0:1, 0:64], SMALL[0:1, 512 + j * 64:512 + j * 64 + 64],
                GR[0:1, j:j + 1], GR[0:1, 8 + j:9 + j], ALU.mult, ALU.add), inc=1)
            bk.guard(3)
            P.waits("pe", [tara])
            tta = transpose(pb[3][0:64, 0:1], SMALL[0:1, 0:64], inc=1, pdim=1, f32=True)
            P.waits("dve", [tta])
            tca = P.op("dve", lambda e: e.tensor_copy(SMB[0:64, 256:257], pb[3][0:64, 0:1]), inc=1)
            bk.reader(3, tca)
            for g in range(8):
                bk.guard(6)
                P.waits("pe", [tca])
                tar = mm(pb[6][0:1, 0:512], SMB[0:64, 256:257], ARENA[0:64, EXo + g * 512:EXo + g * 512 + 512],
                         True, True, inc=1)
                P.waits("dve", [tar])
                trow = P.op("dve", lambda e: e.tensor_copy(SMALL[0:1, 1024:1536], pb[6][0:1, 0:512]), inc=1)
                bk.reader(6, trow)
                bk.guard(7)
                P.waits("pe", [trow])
                tax = mm(pb[7][:, 0:512], ONESR[0:1, :], SMALL[0:1, 1024:1536], True, True, inc=1)
                P.waits("dve", [tax])
                t1_ = P.op("dve", lambda e, g=g: e.tensor_tensor(
                    SRUN[:, g * 512:g * 512 + 512], SRUN[:, g * 512:g * 512 + 512], pb[7][:, 0:512], ALU.mult), inc=1)
                bk.reader(7, t1_)
                t2_ = P.op("dve", lambda e, g=g, j=j: e.scalar_tensor_tensor(
                    SRUN[:, g * 512:g * 512 + 512], ARENA[:, A0 + g * 512:A0 + g * 512 + 512],
                    GV[:, 16 + j:17 + j], SRUN[:, g * 512:g * 512 + 512], ALU.mult, ALU.add), inc=1)
            P.waits("sp", [t2_])
        if dbg in ("sin", "srun"):
            P.waits("dve", [t2_])
            tcv = P.op("dve", lambda e: e.tensor_copy(PH[:, 0:2048], SRUN[:, 0:2048]), inc=1)
            P.waits("sp", [tcv])
            l1 = P.dma("sp", out_fm[0:128, 0:2048], PH[:, 0:2048], sem="dou")
            P.waits("dve", [l1])
            tc2 = P.op("dve", lambda e: e.tensor_copy(PH[:, 0:2048], SRUN[:, 2048:4096]), inc=1)
            P.waits("sp", [tc2])
            lastd = P.dma("sp", out_fm[128:256, 0:2048], PH[:, 0:2048], sem="dou")
            P.wait("sp", lastd)
            return True


        P.phase_fence()
        E0, E1, E2, E3, DTo = 0, 8192, 16384, 24576, 28672
        P.dma_fence("sp")
        tptot = P.op("dve", lambda e: e.memset(SMALL[0:64, 175:176], 1.0), inc=1)
        dt_done2 = []
        for sl in range(8):
            tdx = P.dma("sp", SMALL[0:1, 512:1024], dexp[0:1, sl * 512:sl * 512 + 512], sem="dms")
            bk.guard(4)
            P.waits("pe", [tdx])
            tmmx = mm(pb[4][:, 0:512], ONESR[0:1, :], SMALL[0:1, 512:1024], True, True, inc=1)
            P.waits("dve", [tmmx])
            tdt2 = P.op("dve", lambda e, sl=sl: e.tensor_copy(ARENA[:, DTo + sl * 512:DTo + sl * 512 + 512], pb[4][:, 0:512]), inc=1)
            bk.reader(4, tdt2)
            P.waits("sp", [tdt2])
            dt_done2.append(tdt2)

        for c in range(NCH):
            P.phase_fence()
            pe_pc = P.now("pe")
            P.waits("sp", [P.now("dve"), P.now("act"), P.now("pool"), pe_pc])
            tlx = P.dma("sp", ARENA[:, E0:E0 + 8192].rearrange("p (tc f) -> p tc f", f=4096),
                        xtm_spill[c * 256:(c + 1) * 256, :].rearrange("(tc p) f -> p tc f", p=128), sem="dlx")
            tlp = P.dma("sp", ARENA[:, E3:E3 + 4096], st_spill[c * 128:(c + 1) * 128, :], sem="dlp")
            P.waits("dve", [tptot])
            tpb = P.op("dve", lambda e: e.tensor_copy(SMB[0:64, 1792:1793], SMALL[0:64, 175:176]), inc=1)
            for g in range(8):
                bk.guard(6)
                P.waits("pe", [tpb])
                tmm1 = mm(pb[6][0:1, 0:512], SMB[0:64, 1792:1793], ARENA[0:64, EXo + g * 512:EXo + g * 512 + 512], True, True, inc=1)
                P.waits("dve", [tmm1])
                trw1 = P.op("dve", lambda e: e.tensor_copy(SMALL[0:1, 1024:1536], pb[6][0:1, 0:512]), inc=1)
                bk.reader(6, trw1)
                bk.guard(7)
                P.waits("pe", [trw1])
                tmm2 = mm(pb[7][:, 0:512], ONESR[0:1, :], SMALL[0:1, 1024:1536], True, True, inc=1)
                P.waits("dve", [tmm2, tlp])
                tq1 = P.op("dve", lambda e, g=g: e.tensor_tensor(PH[:, 0:512], SRUN[:, g * 512:g * 512 + 512], pb[7][:, 0:512], ALU.mult), inc=1)
                bk.reader(7, tq1)
                tq2 = P.op("dve", lambda e, g=g: e.tensor_tensor(
                    ARENA[:, E3 + g * 512:E3 + g * 512 + 512], ARENA[:, E3 + g * 512:E3 + g * 512 + 512], PH[:, 0:512], ALU.add), inc=1)
            tck = P.op("act", lambda e, c=c: e.activation(SMALL[0:64, 173:174], DTC[0:64, c, 255:256], AF.Exp), inc=1)
            P.waits("dve", [tck])
            tptot = P.op("dve", lambda e: e.tensor_tensor(SMALL[0:64, 175:176], SMALL[0:64, 175:176], SMALL[0:64, 173:174], ALU.mult), inc=1)
            xdt_toks = []
            for tc_ in range(2):
                for sl in range(8):
                    bno = 6 + (sl % 2)
                    bk.guard(bno)
                    tex = mm(pb[bno][:, 0:512], ARENA[0:64, 38528 + c * 256 + tc_ * 128:38528 + c * 256 + tc_ * 128 + 128],
                             ARENA[0:64, EXo + sl * 512:EXo + sl * 512 + 512], True, True, inc=1)
                    P.waits("dve", [tex, tlx])
                    txd = P.op("dve", lambda e, tc_=tc_, sl=sl, bno=bno: e.tensor_tensor(
                        ARENA[:, E1 + tc_ * 4096 + sl * 512: E1 + tc_ * 4096 + sl * 512 + 512],
                        ARENA[:, E0 + tc_ * 4096 + sl * 512: E0 + tc_ * 4096 + sl * 512 + 512],
                        pb[bno][:, 0:512], ALU.mult), inc=1)
                    bk.reader(bno, txd)
                    xdt_toks.append(txd)
            P.waits("pool", xdt_toks + dt_done2)
            tdx1 = P.op("pool", lambda e: e.tensor_tensor(ARENA[:, E0:E0 + 4096], ARENA[:, E0:E0 + 4096], ARENA[:, DTo:DTo + 4096], ALU.mult), inc=1)
            tdx2 = P.op("pool", lambda e: e.tensor_tensor(ARENA[:, E0 + 4096:E0 + 8192], ARENA[:, E0 + 4096:E0 + 8192], ARENA[:, DTo:DTo + 4096], ALU.mult), inc=1)
            bk.guard(4)
            bk.guard(5)
            tg1 = mm(pb[4][:, 0:256], BCC[:, 0, c, 0:128], BCC[:, 1, c, :], True, True, inc=1)
            tg2 = mm(pb[5][:, 0:256], BCC[:, 0, c, 128:256], BCC[:, 1, c, :], True, True, inc=1)
            P.waits("dve", [tg1, tg2])
            tgm1 = P.op("dve", lambda e: e.tensor_tensor(SMB[:, 0:256], pb[4][:, 0:256], TRIM[:, 0:256], ALU.mult), inc=1)
            tgm2 = P.op("dve", lambda e: e.tensor_tensor(SMB[:, 256:512], pb[5][:, 0:256], TRIM[:, 256:512], ALU.mult), inc=1)
            bk.reader(4, tgm1)
            bk.reader(5, tgm2)
            bk.guard(4)
            ta1 = transpose(pb[4][:, 0:64], DTC[0:64, c, 0:128], inc=1, pdim=64, f32=True)
            bk.guard(5)
            ta2 = transpose(pb[5][:, 0:64], DTC[0:64, c, 128:256], inc=1, pdim=64, f32=True)
            P.waits("dve", [ta1, ta2])
            tac1 = P.op("dve", lambda e: e.tensor_copy(SMALL[:, 640:704], pb[4][:, 0:64]), inc=1)
            tac2 = P.op("dve", lambda e: e.tensor_copy(SMALL[:, 704:768], pb[5][:, 0:64]), inc=1)
            bk.reader(4, tac1)
            bk.reader(5, tac2)
            tei = P.op("act", lambda e, c=c: e.activation(SMALL[0:64, 768:1024], DTC[0:64, c, :], AF.Exp), inc=1)
            P.waits("dve", [tei])
            teda = P.op("dve", lambda e: e.tensor_copy(SMB[0:64, 1536:1792], SMALL[0:64, 768:1024]), inc=1)
            prev_tyo = None
            prev_twh = None
            prev_tex2 = None
            prev_ty = [None, None]
            for g in range(8):
                bk.guard(0)
                bk.guard(1)
                bk.guard(2)
                bk.guard(3)
                for h8 in range(8):
                    h = g * 8 + h8
                    bk.guard(4)
                    P.waits("pe", [tac1, tac2])
                    tb1 = transpose(pb[4][0:1, 0:128], SMALL[:, 640 + h:641 + h], inc=0, f32=True)
                    tb2 = transpose(pb[4][0:1, 128:256], SMALL[:, 704 + h:705 + h], inc=1, f32=True)
                    P.waits("dve", [tb2])
                    trow = P.op("dve", lambda e: e.tensor_copy(SMALL[0:1, 768:1024], pb[4][0:1, 0:256]), inc=1)
                    bk.reader(4, trow)
                    bk.guard(5)
                    P.waits("pe", [trow])
                    ttb = mm(pb[5][:, 0:256], ONESR[0:1, :], SMALL[0:1, 768:1024], True, True, inc=1)
                    P.waits("dve", [ttb, prev_tex2])
                    tt1 = P.op("dve", lambda e, h=h: e.tensor_scalar(SMALL[:, 1024:1280], pb[5][:, 0:256], SMALL[:, 640 + h:641 + h], 0.0, ALU.subtract, ALU.min))
                    tt2 = P.op("dve", lambda e, h=h: e.tensor_scalar(SMALL[:, 1280:1536], pb[5][:, 0:256], SMALL[:, 704 + h:705 + h], 0.0, ALU.subtract, ALU.min), inc=1)
                    bk.reader(5, tt2)
                    P.waits("act", [tt2, prev_twh])
                    tex2 = P.op("act", lambda e: e.activation(SMB[:, 512:1024], SMALL[:, 1024:1536], AF.Exp), inc=1)
                    prev_tex2 = tex2
                    P.waits("pool", [tex2, tgm1, tgm2, prev_tyo])
                    twh = P.op("pool", lambda e: e.tensor_tensor(SMB[:, 1024:1536], SMB[:, 512:1024], SMB[:, 0:512], ALU.mult), inc=1)
                    prev_twh = twh
                    P.waits("pe", [twh, tdx1, tdx2] if h8 == 0 else [twh])
                    for tcm in range(2):
                        mm(pb[tcm][:, h8 * 64:h8 * 64 + 64], IDENT[:, :],
                           ARENA[:, E0 + tcm * 4096 + h * 64:E0 + tcm * 4096 + h * 64 + 64], True, False)
                        for sc in range(2):
                            mm(pb[tcm][:, h8 * 64:h8 * 64 + 64], SMB[:, 1024 + sc * 256 + tcm * 128:1024 + sc * 256 + tcm * 128 + 128],
                               ARENA[:, E1 + sc * 4096 + h * 64:E1 + sc * 4096 + h * 64 + 64], False, sc == 1)
                        tyo = mm(pb[2 + tcm][:, h8 * 64:h8 * 64 + 64], BCC[:, 1, c, tcm * 128:tcm * 128 + 128],
                                 ARENA[:, E3 + h * 64:E3 + h * 64 + 64], True, True,
                                 inc=(1 if tcm == 1 else 0))
                    prev_tyo = tyo
                P.waits("pe", [teda])
                for tcm in range(2):
                    bk.guard(6 + tcm)
                    tee = mm(pb[6 + tcm][:, 0:512], SMB[0:64, 1536 + tcm * 128:1536 + tcm * 128 + 128],
                             ARENA[0:64, EXo + g * 512:EXo + g * 512 + 512], True, True, inc=1)
                    P.waits("act", [tee, prev_ty[tcm]])
                    tpe2 = P.op("act", lambda e, tcm=tcm: e.copy(PH[:, 512 + tcm * 512:512 + tcm * 512 + 512], pb[6 + tcm][:, 0:512]), inc=1)
                    bk.reader(6 + tcm, tpe2)
                    P.waits("dve", [tpe2, tyo])
                    ty1 = P.op("dve", lambda e, g=g, tcm=tcm: e.tensor_tensor(
                        ARENA[:, E2 + tcm * 4096 + g * 512:E2 + tcm * 4096 + g * 512 + 512],
                        pb[2 + tcm][:, 0:512], PH[:, 512 + tcm * 512:512 + tcm * 512 + 512], ALU.mult), inc=1)
                    bk.reader(2 + tcm, ty1)
                    ty2 = P.op("dve", lambda e, g=g, tcm=tcm: e.tensor_tensor(
                        ARENA[:, E2 + tcm * 4096 + g * 512:E2 + tcm * 4096 + g * 512 + 512],
                        ARENA[:, E2 + tcm * 4096 + g * 512:E2 + tcm * 4096 + g * 512 + 512],
                        pb[tcm][:, 0:512], ALU.add), inc=1)
                    bk.reader(tcm, ty2)
                    prev_ty[tcm] = ty1
            y_done = P.now("dve")
            P.waits("pe", [y_done])
            yfe = []
            for half in range(8):
                tc_ = half // 4
                fg = half % 4
                bno = 4 + half % 2
                bk.guard(bno)
                pbf = pb[bno].ap().bitcast(BF16)
                ttp = None
                for k in range(8):
                    fi = fg * 8 + k
                    ttp = transpose(pbf[:, k * 128:k * 128 + 128],
                                    ARENA[:, E2 + tc_ * 4096 + fi * 128:E2 + tc_ * 4096 + fi * 128 + 128],
                                    inc=(1 if k == 7 else 0))
                P.waits("dve", [ttp])
                tev = P.op("dve", lambda e, pbf=pbf, tc_=tc_, fg=fg: e.tensor_copy(
                    ARENA[:, E0 + fg * 2048 + tc_ * 128 * 8:E0 + fg * 2048 + tc_ * 128 * 8 + 1024] if False else
                    ARENA[:, E0:E0 + 8192].rearrange("p (f tc t) -> p f tc t", tc=2, t=128)[:, fg * 8:fg * 8 + 8, tc_, :],
                    pbf[:, 0:1024].rearrange("p (f t) -> p f t", t=128)), inc=1)
                bk.reader(bno, tev)
                yfe.append(tev)
            P.waits("sp", [yfe[-1]])
            tgl = P.dma("sp", ARENA[:, E1:E1 + 8192].rearrange("p (f t) -> p f t", t=256),
                        gate_spill[:, c * 256:(c + 1) * 256].rearrange("(f p) t -> p f t", p=128), sem="dlg")
            P.waits("dve", yfe + [tgl])
            thg = P.op("dve", lambda e: e.tensor_tensor(ARENA[:, E0:E0 + 8192], ARENA[:, E0:E0 + 8192], ARENA[:, E1:E1 + 8192], ALU.mult), inc=1)
            P.waits("act", [thg])
            tsq = P.op("act", lambda e: e.activation(ARENA[:, E2:E2 + 8192], ARENA[:, E0:E0 + 8192], AF.Square), inc=1)
            bk.guard(6)
            P.waits("pe", [tsq])
            tst2 = None
            for fi in range(32):
                tst2 = mm(pb[6][0:1, 0:256], ONESC[:, :], ARENA[:, E2 + fi * 256:E2 + fi * 256 + 256],
                          fi == 0, fi == 31, inc=(1 if fi == 31 else 0))
            P.waits("dve", [tst2])
            tvr = P.op("dve", lambda e: e.tensor_scalar(SMALL[0:1, 0:256], pb[6][0:1, 0:256], 1.0 / DIN, None, ALU.mult), inc=1)
            bk.reader(6, tvr)
            P.waits("act", [tvr])
            P.op("act", lambda e: e.activation(SMALL[0:1, 0:256], SMALL[0:1, 0:256], AF.Ln, bias=KON[0:1, 0:1]))
            trst = P.op("act", lambda e: e.activation(SMALL[0:1, 0:256], SMALL[0:1, 0:256], AF.Exp, scale=-0.5), inc=1)
            bk.guard(7)
            P.waits("pe", [trst])
            trb = mm(pb[7][:, 0:256], ONESR[0:1, :], SMALL[0:1, 0:256], True, True, inc=1)
            tyn = None
            P.waits("dve", [trb])
            for fi in range(32):
                tyn = P.op("dve", lambda e, fi=fi: e.scalar_tensor_tensor(
                    ARENA[:, E1 + fi * 256:E1 + fi * 256 + 256],
                    ARENA[:, E0 + fi * 256:E0 + fi * 256 + 256],
                    BIAS[:, 172 + fi:173 + fi], pb[7][:, 0:256], ALU.mult, ALU.mult), inc=(1 if fi == 31 else 0))
            bk.reader(7, tyn)
            def fill(dM, c=c, tyn=tyn):
                b = next_bank()
                WBa, tca, wia = load_panel(opT[0:2048, dM * 128:(dM + 1) * 128], 16)
                P.waits("pe", [tca, tyn])
                tpa = None
                for kc in range(16):
                    tpa = mm(pb[b][:, 0:256], WBa[:, kc * 128:(kc + 1) * 128],
                             ARENA[:, E1 + kc * 256:E1 + kc * 256 + 256], kc == 0, False,
                             inc=(1 if kc == 15 else 0))
                panel_used(wia, tpa)
                WBb, tcb, wib = load_panel(opT[2048:4096, dM * 128:(dM + 1) * 128], 16)
                P.waits("pe", [tcb])
                tpe3 = None
                for kc in range(16):
                    tpe3 = mm(pb[b][:, 0:256], WBb[:, kc * 128:(kc + 1) * 128],
                              ARENA[:, E1 + 4096 + kc * 256:E1 + 4096 + kc * 256 + 256], False, False,
                              inc=(1 if kc == 15 else 0))
                panel_used(wib, tpe3)
                tpe4 = mm(pb[b][:, 0:256], IDENT[:, :], STRM[:, dM, c * 256:c * 256 + 256], False, True, inc=1)
                return b, tpe4

            ln_tile(c * 256, 256, fill)
        return False


    def emit_dump():
        last = None
        for f in range(NF):
            P.waits("dve", [P.now("pool"), P.now("act"), last])
            tcv = P.op("dve", lambda e, f=f: e.tensor_copy(PH[:, 0:2048], STRM[:, f, :]), inc=1)
            P.waits("sp", [tcv])
            last = P.dma("sp", out_fm[f * 128:(f + 1) * 128, :], PH[:, 0:2048])
        P.wait("sp", last)

    blocks = list(range(1, nblocks + 1)) if isinstance(nblocks, int) else list(nblocks)
    stop = False
    if 1 in blocks:
        stop = emit_l0a()
    if 2 in blocks and not stop:
        stop = emit_l0f()
    if 3 in blocks and not stop:
        stop = emit_l1a()
    if 4 in blocks and not stop:
        stop = emit_l1f(final=(4 == max(blocks) and len(blocks) == 4))
    if not stop:
        emit_dump()

    with nc.Block() as block:
        P.replay(block)
    es.close()
    return nc


_prog_cache = {}


def _get_prog(nblocks, dbg=None, dbg2=None):
    key = (tuple(nblocks) if not isinstance(nblocks, int) else nblocks, dbg, dbg2)
    if key not in _prog_cache:
        _prog_cache[key] = build_program(nblocks, dbg, dbg2)
    return _prog_cache[key]


def _prep_shared(I):
    f32 = np.float32
    g = lambda k: np.asarray(I[k], f32)
    A = {}
    A["wuT"] = np.ascontiguousarray(g("L0_attn_Wu").T)
    A["wvT"] = np.ascontiguousarray(g("L0_attn_Wv").T)
    A["wg1T"] = np.ascontiguousarray(g("L0_attn_Wg1").T)
    A["wg2T"] = np.ascontiguousarray(g("L0_attn_Wg2").T)
    bl = np.zeros((128, 116), f32)
    bl[:, 0:16] = g("L0_attn_bu").reshape(16, 128).T
    bl[:, 16:32] = g("L0_attn_bv").reshape(16, 128).T
    bl[:, 32:48] = g("L0_attn_bg2").reshape(16, 128).T
    bl[:, 48:64] = g("L0_attn_conv_b").reshape(16, 128).T
    cw = g("L0_attn_conv_w")[:, 0, :]
    for k in range(3):
        bl[:, 64 + 16 * k:80 + 16 * k] = cw[:, k].reshape(16, 128).T
    bl[:, 112:116] = g("L0_attn_bg1").reshape(4, 128).T
    A["bL0a"] = bl
    A["gw0"] = np.ascontiguousarray(g("L0_ffn_gate_w"))
    A["w10"] = np.ascontiguousarray(g("L0_ffn_w1"))
    b10 = np.zeros((128, 80), f32)
    for e in range(8):
        b10[:, e * 8:(e + 1) * 8] = g("L0_ffn_b1")[e].reshape(8, 128).T
    b10[:, 64:80] = g("L0_ffn_shared_b1").reshape(16, 128).T
    A["b10"] = b10
    A["w20"] = np.ascontiguousarray(g("L0_ffn_w2"))
    A["b20"] = np.ascontiguousarray(g("L0_ffn_b2"))
    A["sw1"] = np.ascontiguousarray(g("L0_ffn_shared_w1"))
    A["sw2"] = np.ascontiguousarray(g("L0_ffn_shared_w2"))
    A["sb2c"] = g("L0_ffn_shared_b2").reshape(16, 128).T
    ip = np.zeros((2048, PROJ_PAD), f32)
    ip[:, 0:8512] = g("L1_attn_in_proj_w").T
    A["ipT"] = ip
    cw1 = np.zeros((128, 136), f32)
    cb1 = np.zeros((128, 34), f32)
    wconv = g("L1_attn_conv_w")
    bconv = g("L1_attn_conv_b")
    for ch in range(34):
        for k in range(4):
            cw1[:, ch * 4 + k] = wconv[ch * 128:(ch + 1) * 128, k]
        cb1[:, ch] = bconv[ch * 128:(ch + 1) * 128]
    A["cw1"] = cw1
    A["cb1"] = cb1
    A["dtb"] = g("L1_attn_dt_bias").reshape(64, 1)
    A["alog"] = g("L1_attn_A_log").reshape(64, 1)
    A["nwc"] = g("L1_attn_norm_w").reshape(32, 128).T
    A["dexp"] = np.repeat(g("L1_attn_D"), 64).reshape(1, DIN)
    A["opT"] = np.ascontiguousarray(g("L1_attn_out_proj_w").T)
    A["gw1"] = np.ascontiguousarray(g("L1_ffn_gate_w").T)
    A["wdT"] = np.ascontiguousarray(np.transpose(g("L1_ffn_wd"), (0, 2, 1)))
    bd1 = np.zeros((128, 32), f32)
    for e in range(8):
        bd1[:, e * 4:(e + 1) * 4] = g("L1_ffn_bd")[e].reshape(4, 128).T
    A["bd1"] = bd1
    A["wuT1"] = np.ascontiguousarray(np.transpose(g("L1_ffn_wu"), (0, 2, 1)))
    A["bu1"] = np.ascontiguousarray(g("L1_ffn_bu"))
    A["wsdT"] = np.ascontiguousarray(g("L1_ffn_ws_d").T)
    A["bsdc"] = g("L1_ffn_bs_d").reshape(4, 128).T
    A["wsuT"] = np.ascontiguousarray(g("L1_ffn_ws_u").T)
    A["bsuc"] = g("L1_ffn_bs_u").reshape(16, 128).T
    A["identf"] = np.eye(128, dtype=f32)
    em = np.zeros((64, DIN), f32)
    for h in range(64):
        em[h, h * 64:(h + 1) * 64] = 1.0
    A["expmf"] = em
    tr = np.zeros((128, 512), f32)
    spi = np.arange(128)[:, None]
    tti = np.arange(256)[None, :]
    tr[:, 0:256] = (spi <= tti).astype(f32)
    tr[:, 256:512] = (spi + 128 <= tti).astype(f32)
    A["trif"] = tr
    sel = np.zeros((8, 1024), f32)
    for e in range(8):
        sel[e, e * 128:(e + 1) * 128] = 1.0
    A["selmf"] = sel
    return A


def _prep_core(I, c):
    f32 = np.float32
    b, q = c // 4, c % 4
    x = np.asarray(I["L0_attn_x"], f32)
    seg = x[b, q * T:(q + 1) * T, :]
    xt = np.zeros((D, T + 2), f32)
    xt[:, 1:T + 1] = seg.T
    if q > 0:
        xt[:, 0] = x[b, q * T - 1, :]
    if q < 3:
        xt[:, T + 1] = x[b, (q + 1) * T, :]
    mc = np.zeros((128, 8), f32)
    mt = np.zeros((128, 8), f32)
    gc = np.zeros((128, 8), f32)
    gr = np.zeros((1, 16), f32)
    for j in range(8):
        inc = (j // 4 == b) and (j % 4 < q)
        mc[:, j] = 1.0 if inc else 0.0
        gc[:, j] = 1.0 if inc else 0.0
        gr[0, j] = 1.0 if inc else 0.0
        gr[0, 8 + j] = 0.0 if inc else 1.0
        mt[:, j] = 1.0 if (q > 0 and j == c - 1) else 0.0
    return {"xT": xt, "mcum": mc, "mtail": mt, "gcolf": gc, "grow": gr}


def kernel(_nblocks=4, _dbg=None, _dbg2=None, **inputs):
    nc = _get_prog(_nblocks, _dbg, _dbg2)
    shared = _prep_shared(inputs)
    in_maps = []
    for c in range(8):
        m = dict(shared)
        m.update(_prep_core(inputs, c))
        in_maps.append(m)
    res = run_bass_kernel_spmd(nc, in_maps, core_ids=list(range(8)))
    out = np.zeros((2, 8192, 2048), np.float32)
    for c in range(8):
        b, q = c // 4, c % 4
        out[b, q * T:(q + 1) * T, :] = res.results[c]["out_fm"].T
    return out
